# version 05 (final)
# baseline (speedup 1.0000x reference)
import numpy as np
from contextlib import ExitStack
import ml_dtypes
import concourse.bass as bass
import concourse.mybir as mybir
from concourse.bass_utils import run_bass_kernel_spmd
from concourse.alu_op_type import AluOpType as ALU

F32 = mybir.dt.float32
BF16 = mybir.dt.bfloat16
I32 = mybir.dt.int32
AF = mybir.ActivationFunctionType
BF = ml_dtypes.bfloat16

B, L, D = 4, 8192, 2048
E, N, DT, DC = 4096, 16, 128, 4
EH = E // 2
LH = L // 2
MLP = 16384
EPS = 1e-5


class Prog:

    def __init__(self, nc):
        self.nc = nc
        self.items = []
        self.acc = 0

    def emit(self, eng, fn, inc=1, wait=True):
        self.items.append((eng, fn, self.acc if wait else None, inc))
        self.acc += inc

    def dma(self, q, out, in_):
        self.emit(q, lambda e: e.dma_start(out=out, in_=in_), 16)

    def finalize(self, block):
        items = self.items
        total = self.acc

        def run(e, name):
            last_wait = -1
            for (eng, fn, wait_at, inc) in items:
                if eng != name:
                    continue
                if wait_at is not None and wait_at > last_wait:
                    e.wait_ge(self.S, wait_at)
                    last_wait = wait_at
                inst = fn(e)
                if inc:
                    inst.then_inc(self.S, inc)
            if name == "sync":
                e.wait_ge(self.S, total)

        @block.sync
        def _(e):
            run(e, "sync")

        @block.vector
        def _(e):
            run(e, "vector")

        @block.scalar
        def _(e):
            run(e, "scalar")

        @block.gpsimd
        def _(e):
            run(e, "gpsimd")

        @block.tensor
        def _(e):
            run(e, "tensor")


_NC_CACHE = {}


def _run(key, build_fn, in_maps):
    if key not in _NC_CACHE:
        nc = bass.Bass("TRN2", target_bir_lowering=False)
        build_fn(nc)
        _NC_CACHE[key] = nc
    res = run_bass_kernel_spmd(_NC_CACHE[key], in_maps, core_ids=list(range(8)))
    return res.results


def _mm_chain(p, psum, lhsT_tiles, rhs_tiles):
    n = len(lhsT_tiles)
    for k in range(n):
        lt, rt = lhsT_tiles[k], rhs_tiles[k]
        last = (k == n - 1)
        p.emit("tensor",
               (lambda lt=lt, rt=rt, s=(k == 0), t=last:
                (lambda e: e.matmul(out=psum, lhsT=lt, rhs=rt, start=s, stop=t)))(),
               inc=(1 if last else 0), wait=(k == 0))



def _consts(nc, p, st_, values, name="cst"):
    buf = st_.enter_context(nc.sbuf_tensor(name, [128, max(8, len(values))], F32))
    for i, v in enumerate(values):
        p.emit("vector", (lambda i=i, v=v: (lambda e: e.memset(
            buf[:, i:i + 1], v)))())
        nc.const_aps.register(F32, v, buf[:, i:i + 1])


def build_dequant(nc):
    gp = nc.dram_tensor("gp", [2048, 1024], I32, kind="ExternalInput").ap()
    up = nc.dram_tensor("up", [2048, 1024], I32, kind="ExternalInput").ap()
    dp = nc.dram_tensor("dp", [256, 8192], I32, kind="ExternalInput").ap()
    gs = nc.dram_tensor("gs", [2048, 2048], F32, kind="ExternalInput").ap()
    us = nc.dram_tensor("us", [2048, 2048], F32, kind="ExternalInput").ap()
    ds = nc.dram_tensor("ds", [256, 16384], F32, kind="ExternalInput").ap()
    og = nc.dram_tensor("og", [2048, 2048], BF16, kind="ExternalOutput").ap()
    ou = nc.dram_tensor("ou", [2048, 2048], BF16, kind="ExternalOutput").ap()
    od = nc.dram_tensor("od", [256, 16384], BF16, kind="ExternalOutput").ap()

    WB_ = 1024
    KB = 2 * WB_
    with ExitStack() as _st:
        w = _st.enter_context(nc.sbuf_tensor("w", [128, WB_], I32))
        sc = _st.enter_context(nc.sbuf_tensor("sc", [128, KB], F32))
        ta = _st.enter_context(nc.sbuf_tensor("ta", [128, WB_], I32))
        tb_ = _st.enter_context(nc.sbuf_tensor("tb_", [128, WB_], I32))
        tf = _st.enter_context(nc.sbuf_tensor("tf", [128, WB_], F32))
        tg = _st.enter_context(nc.sbuf_tensor("tg", [128, WB_], F32))
        v = _st.enter_context(nc.sbuf_tensor("v", [128, KB], F32))
        vb = _st.enter_context(nc.sbuf_tensor("vb", [128, KB], BF16))
        S = _st.enter_context(nc.semaphore("S"))
        block = _st.enter_context(nc.Block())
        p = Prog(nc)
        p.S = S
        vi = v.rearrange("p (m two) -> p two m", two=2)

        def deq(packed, scales, out, rows, words):
            ntiles = rows // 128
            nch = words // WB_
            for t in range(ntiles):
                for ch in range(nch):
                    wsl = slice(WB_ * ch, WB_ * (ch + 1))
                    ksl = slice(KB * ch, KB * (ch + 1))
                    p.dma("sync", w[:, :],
                          packed[128 * t:128 * (t + 1), wsl])
                    p.dma("sync", sc[:, :],
                          scales[128 * t:128 * (t + 1), ksl])
                    for hi in range(2):
                        if hi == 0:
                            p.emit("vector", lambda e: e.tensor_scalar(
                                out=ta[:], in0=w[:],
                                scalar1=15, scalar2=None, op0=ALU.bitwise_and))
                        else:
                            p.emit("vector", lambda e: e.tensor_scalar(
                                out=ta[:], in0=w[:],
                                scalar1=4, scalar2=15,
                                op0=ALU.logical_shift_right, op1=ALU.bitwise_and))
                        p.emit("vector", lambda e: e.tensor_scalar(
                            out=tb_[:], in0=ta[:],
                            scalar1=7, scalar2=None, op0=ALU.bitwise_and))
                        p.emit("vector", lambda e: e.tensor_scalar(
                            out=tf[:], in0=tb_[:],
                            scalar1=4.0, scalar2=0.0,
                            op0=ALU.subtract, op1=ALU.max))
                        p.emit("vector", lambda e: e.tensor_scalar(
                            out=tg[:], in0=tb_[:],
                            scalar1=0.0, scalar2=None, op0=ALU.add))
                        p.emit("vector", lambda e: e.tensor_add(
                            out=tf[:], in0=tf[:], in1=tg[:]))
                        p.emit("vector", lambda e: e.tensor_scalar(
                            out=tg[:], in0=tb_[:],
                            scalar1=6.0, scalar2=0.0,
                            op0=ALU.subtract, op1=ALU.max))
                        p.emit("vector", lambda e: e.scalar_tensor_tensor(
                            out=tf[:], in0=tg[:], scalar=2.0,
                            in1=tf[:], op0=ALU.mult, op1=ALU.add))
                        p.emit("vector", lambda e: e.tensor_scalar(
                            out=tb_[:], in0=ta[:],
                            scalar1=8, scalar2=None, op0=ALU.bitwise_and))
                        p.emit("vector", lambda e: e.tensor_scalar(
                            out=tg[:], in0=tb_[:],
                            scalar1=-0.25, scalar2=1.0,
                            op0=ALU.mult, op1=ALU.add))
                        p.emit("vector", (lambda hi=hi: (lambda e: e.tensor_tensor(
                            out=vi[:, hi, :], in0=tf[:],
                            in1=tg[:], op=ALU.mult)))())
                    p.emit("vector", lambda e: e.scalar_tensor_tensor(
                        out=vb[:], in0=v[:], scalar=0.5,
                        in1=sc[:], op0=ALU.mult, op1=ALU.mult))
                    p.dma("gpsimd", out[128 * t:128 * (t + 1), ksl], vb[:])

        deq(gp, gs, og, 2048, 1024)
        deq(up, us, ou, 2048, 1024)
        deq(dp, ds, od, 256, 8192)
        p.finalize(block)


def build_ln(nc):
    x = nc.dram_tensor("x", [4096, 2048], F32, kind="ExternalInput").ap()
    nw = nc.dram_tensor("nw", [128, 2048], F32, kind="ExternalInput").ap()
    o1 = nc.dram_tensor("o1", [4096, 2048], BF16, kind="ExternalOutput").ap()
    o2 = nc.dram_tensor("o2", [4096, 2048], BF16, kind="ExternalOutput").ap()
    xt = x.rearrange("(t p) c -> t p c", p=128)
    o1t = o1.rearrange("(t p) c -> t p c", p=128)
    o2t = o2.rearrange("(t p) c -> t p c", p=128)

    with ExitStack() as _st:
        xs = _st.enter_context(nc.sbuf_tensor("xs", [128, 2048], F32))
        nws = _st.enter_context(nc.sbuf_tensor("nws", [128, 2048], F32))
        sq = _st.enter_context(nc.sbuf_tensor("sq", [128, 2048], F32))
        x1 = _st.enter_context(nc.sbuf_tensor("x1", [128, 2048], F32))
        b1 = _st.enter_context(nc.sbuf_tensor("b1", [128, 2048], BF16))
        b2 = _st.enter_context(nc.sbuf_tensor("b2", [128, 2048], BF16))
        st = _st.enter_context(nc.sbuf_tensor("st", [128, 8], F32))
        S = _st.enter_context(nc.semaphore("S"))
        block = _st.enter_context(nc.Block())
        p = Prog(nc)
        p.S = S
        _consts(nc, p, _st, [EPS, -0.5, 1.0 / 2048])
        p.dma("sync", nws[:], nw[:, :])
        for t in range(32):
            p.dma("sync", xs[:], xt[t])
            p.emit("vector", lambda e: e.reduce_sum(
                out=st[:, 0:1], in_=xs[:], axis=mybir.AxisListType.X))
            p.emit("vector", lambda e: e.tensor_scalar(
                out=st[:, 1:2], in0=st[:, 0:1], scalar1=1.0 / 2048,
                scalar2=None, op0=ALU.mult))
            p.emit("scalar", lambda e: e.activation(
                out=sq[:], in_=xs[:], func=AF.Square))
            p.emit("vector", lambda e: e.reduce_sum(
                out=st[:, 2:3], in_=sq[:], axis=mybir.AxisListType.X))
            p.emit("vector", lambda e: e.tensor_mul(
                out=st[:, 3:4], in0=st[:, 1:2], in1=st[:, 1:2]))
            p.emit("vector", lambda e: e.tensor_scalar(
                out=st[:, 4:5], in0=st[:, 2:3], scalar1=1.0 / 2048,
                scalar2=None, op0=ALU.mult))
            p.emit("vector", lambda e: e.tensor_sub(
                out=st[:, 5:6], in0=st[:, 4:5], in1=st[:, 3:4]))
            p.emit("scalar", lambda e: e.activation(
                out=st[:, 6:7], in_=st[:, 5:6], func=AF.Ln, bias=EPS))
            p.emit("scalar", lambda e: e.activation(
                out=st[:, 7:8], in_=st[:, 6:7], func=AF.Exp, scale=-0.5))
            p.emit("vector", lambda e: e.tensor_scalar(
                out=x1[:], in0=xs[:], scalar1=st[:, 1:2], scalar2=st[:, 7:8],
                op0=ALU.subtract, op1=ALU.mult))
            p.emit("vector", lambda e: e.tensor_copy(out=b1[:], in_=x1[:]))
            p.emit("scalar", lambda e: e.activation(
                out=sq[:], in_=x1[:], func=AF.Square))
            p.emit("vector", lambda e: e.reduce_sum(
                out=st[:, 2:3], in_=sq[:], axis=mybir.AxisListType.X))
            p.emit("scalar", lambda e: e.activation(
                out=st[:, 6:7], in_=st[:, 2:3], func=AF.Ln, bias=EPS,
                scale=1.0 / 2048))
            p.emit("scalar", lambda e: e.activation(
                out=st[:, 7:8], in_=st[:, 6:7], func=AF.Exp, scale=-0.5))
            p.emit("vector", lambda e: e.tensor_scalar(
                out=x1[:], in0=x1[:], scalar1=st[:, 7:8], scalar2=None,
                op0=ALU.mult))
            p.emit("vector", lambda e: e.tensor_tensor(
                out=b2[:], in0=x1[:], in1=nws[:], op=ALU.mult))
            p.dma("gpsimd", o1t[t], b1[:])
            p.dma("gpsimd", o2t[t], b2[:])
        p.finalize(block)


def build_proj(nc, silu):
    wT = nc.dram_tensor("wT", [2048, 2048], F32, kind="ExternalInput").ap()
    xT = nc.dram_tensor("xT", [2048, 8192], BF16, kind="ExternalInput").ap()
    if silu:
        o = nc.dram_tensor("o", [2048, 8192], BF16, kind="ExternalOutput").ap()
    else:
        o = nc.dram_tensor("o", [2048, 8192], F32, kind="ExternalOutput").ap()
    wt = wT.rearrange("(t p) c -> t p c", p=128)
    xt = xT.rearrange("(t p) c -> t p c", p=128)
    ot = o.rearrange("(t p) c -> t p c", p=128)

    with ExitStack() as _st:
        wf = _st.enter_context(nc.sbuf_tensor("wf", [128, 2048], F32))
        wb = _st.enter_context(nc.sbuf_tensor("wb", [128, 16, 2048], BF16))
        xb = _st.enter_context(nc.sbuf_tensor("xb", [128, 16, 512], BF16))
        ev = _st.enter_context(nc.sbuf_tensor("ev", [128, 512], F32 if not silu else BF16))
        ps = _st.enter_context(nc.psum_tensor("ps", [128, 512], F32))
        S = _st.enter_context(nc.semaphore("S"))
        block = _st.enter_context(nc.Block())
        p = Prog(nc)
        p.S = S
        for k in range(16):
            p.dma("sync", wf[:, :], wt[k])
            p.emit("vector", (lambda k=k: (lambda e: e.tensor_copy(
                out=wb[:, k, :], in_=wf[:, :])))())
        for c in range(16):
            for k in range(16):
                p.dma("sync", xb[:, k, :], xt[k][:, 512 * c:512 * (c + 1)])
            for et in range(16):
                _mm_chain(p, ps[:, :],
                          [wb[:, k, 128 * et:128 * (et + 1)] for k in range(16)],
                          [xb[:, k, :] for k in range(16)])
                if silu:
                    p.emit("scalar", lambda e: e.activation(
                        out=ev[:], in_=ps[:, :], func=AF.Silu))
                else:
                    p.emit("scalar", lambda e: e.copy(out=ev[:], in_=ps[:, :]))
                p.dma("gpsimd", ot[et][:, 512 * c:512 * (c + 1)], ev[:])
        p.finalize(block)


def build_conv(nc):
    u = nc.dram_tensor("u", [2048, 8192], F32, kind="ExternalInput").ap()
    cw = nc.dram_tensor("cw", [128, 64], F32, kind="ExternalInput").ap()
    cb = nc.dram_tensor("cb", [128, 16], F32, kind="ExternalInput").ap()
    o = nc.dram_tensor("o", [2048, 8192], BF16, kind="ExternalOutput").ap()
    ut = u.rearrange("(t p) c -> t p c", p=128)
    ot = o.rearrange("(t p) c -> t p c", p=128)

    with ExitStack() as _st:
        us = _st.enter_context(nc.sbuf_tensor("us", [128, 8200], F32))
        cws = _st.enter_context(nc.sbuf_tensor("cws", [128, 64], F32))
        cbs = _st.enter_context(nc.sbuf_tensor("cbs", [128, 16], F32))
        a0 = _st.enter_context(nc.sbuf_tensor("a0", [128, 8192], F32))
        a1 = _st.enter_context(nc.sbuf_tensor("a1", [128, 8192], F32))
        ob = _st.enter_context(nc.sbuf_tensor("ob", [128, 8192], BF16))
        S = _st.enter_context(nc.semaphore("S"))
        block = _st.enter_context(nc.Block())
        p = Prog(nc)
        p.S = S
        p.dma("sync", cws[:], cw[:, :])
        p.dma("sync", cbs[:], cb[:, :])
        p.emit("vector", lambda e: e.memset(us[:, 0:3], 0.0))
        for t in range(16):
            p.dma("sync", us[:, 3:8195], ut[t])
            p.emit("vector", (lambda t=t: (lambda e: e.tensor_scalar(
                out=a0[:], in0=us[:, 0:8192], scalar1=cws[:, 4 * t:4 * t + 1],
                scalar2=None, op0=ALU.mult)))())
            for k in range(1, 4):
                src, dst = (a0, a1) if k % 2 else (a1, a0)
                p.emit("vector", (lambda t=t, k=k, src=src, dst=dst:
                       (lambda e: e.scalar_tensor_tensor(
                           out=dst[:], in0=us[:, k:k + 8192],
                           scalar=cws[:, 4 * t + k:4 * t + k + 1],
                           in1=src[:], op0=ALU.mult, op1=ALU.add)))())
            p.emit("scalar", (lambda t=t: (lambda e: e.activation(
                out=ob[:], in_=a1[:], func=AF.Silu,
                bias=cbs[:, t:t + 1])))())
            p.dma("gpsimd", ot[t], ob[:])
        p.finalize(block)


def build_tbc(nc):
    u = nc.dram_tensor("u", [4096, 4096], BF16, kind="ExternalInput").ap()
    w1 = nc.dram_tensor("w1", [4096, 128], F32, kind="ExternalInput").ap()
    wb = nc.dram_tensor("wb", [4096, 16], F32, kind="ExternalInput").ap()
    wc = nc.dram_tensor("wc", [4096, 16], F32, kind="ExternalInput").ap()
    t1 = nc.dram_tensor("t1", [128, 4096], F32, kind="ExternalOutput").ap()
    bo = nc.dram_tensor("bo", [16, 4096], F32, kind="ExternalOutput").ap()
    co = nc.dram_tensor("co", [16, 4096], F32, kind="ExternalOutput").ap()
    ut = u.rearrange("(t p) c -> t p c", p=128)
    w1t = w1.rearrange("(t p) c -> t p c", p=128)
    wbt = wb.rearrange("(t p) c -> t p c", p=128)
    wct = wc.rearrange("(t p) c -> t p c", p=128)

    with ExitStack() as _st:
        w1f = _st.enter_context(nc.sbuf_tensor("w1f", [128, 32, 128], F32))
        w1b = _st.enter_context(nc.sbuf_tensor("w1b", [128, 32, 128], BF16))
        wbf = _st.enter_context(nc.sbuf_tensor("wbf", [128, 32, 32], F32))
        wbb = _st.enter_context(nc.sbuf_tensor("wbb", [128, 32, 32], BF16))
        ub = _st.enter_context(nc.sbuf_tensor("ub", [128, 32, 512], BF16))
        ev = _st.enter_context(nc.sbuf_tensor("ev", [128, 512], F32))
        ps = _st.enter_context(nc.psum_tensor("ps", [128, 512], F32))
        psb = _st.enter_context(nc.psum_tensor("psb", [32, 512], F32))
        S = _st.enter_context(nc.semaphore("S"))
        block = _st.enter_context(nc.Block())
        p = Prog(nc)
        p.S = S
        for k in range(32):
            p.dma("sync", w1f[:, k, :], w1t[k])
            p.dma("sync", wbf[:, k, 0:16], wbt[k])
            p.dma("sync", wbf[:, k, 16:32], wct[k])
        for k in range(32):
            p.emit("vector", (lambda k=k: (lambda e: e.tensor_copy(
                out=w1b[:, k, :], in_=w1f[:, k, :])))())
            p.emit("vector", (lambda k=k: (lambda e: e.tensor_copy(
                out=wbb[:, k, :], in_=wbf[:, k, :])))())
        for c in range(8):
            for k in range(32):
                p.dma("sync", ub[:, k, :], ut[k][:, 512 * c:512 * (c + 1)])
            _mm_chain(p, ps[:, :], [w1b[:, k, :] for k in range(32)],
                      [ub[:, k, :] for k in range(32)])
            p.emit("scalar", lambda e: e.copy(out=ev[:], in_=ps[:, :]))
            p.dma("gpsimd", t1[:, 512 * c:512 * (c + 1)], ev[:])
            _mm_chain(p, psb[:, :], [wbb[:, k, :] for k in range(32)],
                      [ub[:, k, :] for k in range(32)])
            p.emit("scalar", lambda e: e.copy(out=ev[0:32, :], in_=psb[:, :]))
            p.dma("gpsimd", bo[:, 512 * c:512 * (c + 1)], ev[0:16, :])
            p.dma("gpsimd", co[:, 512 * c:512 * (c + 1)], ev[16:32, :])
        p.finalize(block)


def build_delta(nc):
    t1 = nc.dram_tensor("t1", [128, 8192], F32, kind="ExternalInput").ap()
    w2 = nc.dram_tensor("w2", [128, 2048], F32, kind="ExternalInput").ap()
    bd = nc.dram_tensor("bd", [128, 16], F32, kind="ExternalInput").ap()
    o = nc.dram_tensor("o", [2048, 8192], F32, kind="ExternalOutput").ap()
    ot = o.rearrange("(t p) c -> t p c", p=128)

    with ExitStack() as _st:
        t1f = _st.enter_context(nc.sbuf_tensor("t1f", [128, 8192], F32))
        t1b = _st.enter_context(nc.sbuf_tensor("t1b", [128, 8192], BF16))
        w2f = _st.enter_context(nc.sbuf_tensor("w2f", [128, 2048], F32))
        w2b = _st.enter_context(nc.sbuf_tensor("w2b", [128, 2048], BF16))
        bds = _st.enter_context(nc.sbuf_tensor("bds", [128, 16], F32))
        rl = _st.enter_context(nc.sbuf_tensor("rl", [128, 512], F32))
        ab = _st.enter_context(nc.sbuf_tensor("ab", [128, 512], F32))
        dl = _st.enter_context(nc.sbuf_tensor("dl", [128, 512], F32))
        ps = _st.enter_context(nc.psum_tensor("ps", [128, 512], F32))
        S = _st.enter_context(nc.semaphore("S"))
        block = _st.enter_context(nc.Block())
        p = Prog(nc)
        p.S = S
        _consts(nc, p, _st, [-1.0, 1.0])
        p.dma("sync", t1f[:], t1[:, :])
        p.dma("sync", w2f[:], w2[:, :])
        p.dma("sync", bds[:], bd[:, :])
        p.emit("vector", lambda e: e.tensor_copy(out=t1b[:], in_=t1f[:]))
        p.emit("vector", lambda e: e.tensor_copy(out=w2b[:], in_=w2f[:]))
        for et in range(16):
            for c in range(16):
                _mm_chain(p, ps[:, :], [w2b[:, 128 * et:128 * (et + 1)]],
                          [t1b[:, 512 * c:512 * (c + 1)]])
                p.emit("scalar", (lambda et=et: (lambda e: e.activation(
                    out=rl[:], in_=ps[:, :], func=AF.Relu,
                    bias=bds[:, et:et + 1])))())
                p.emit("scalar", (lambda et=et: (lambda e: e.activation(
                    out=ab[:], in_=ps[:, :], func=AF.Abs,
                    bias=bds[:, et:et + 1])))())
                p.emit("scalar", lambda e: e.activation(
                    out=ab[:], in_=ab[:], func=AF.Exp, scale=-1.0))
                p.emit("scalar", lambda e: e.activation(
                    out=ab[:], in_=ab[:], func=AF.Ln, bias=1.0))
                p.emit("vector", lambda e: e.tensor_add(
                    out=dl[:], in0=rl[:], in1=ab[:]))
                p.dma("gpsimd", ot[et][:, 512 * c:512 * (c + 1)], dl[:])
        p.finalize(block)


def build_scan(nc):
    dl = nc.dram_tensor("dl", [2048, 8192], F32, kind="ExternalInput").ap()
    u = nc.dram_tensor("u", [2048, 8192], BF16, kind="ExternalInput").ap()
    bt = nc.dram_tensor("bt", [16, 8192], F32, kind="ExternalInput").ap()
    ct = nc.dram_tensor("ct", [16, 8192], F32, kind="ExternalInput").ap()
    al = nc.dram_tensor("al", [128, 256], F32, kind="ExternalInput").ap()
    wd = nc.dram_tensor("wd", [128, 16], F32, kind="ExternalInput").ap()
    sk = nc.dram_tensor("sk", [2048, 8192], BF16, kind="ExternalInput").ap()
    o = nc.dram_tensor("o", [2048, 8192], BF16, kind="ExternalOutput").ap()
    dlt = dl.rearrange("(t p) c -> t p c", p=128)
    ut = u.rearrange("(t p) c -> t p c", p=128)
    skt = sk.rearrange("(t p) c -> t p c", p=128)
    ot = o.rearrange("(t p) c -> t p c", p=128)
    LQ = 4096

    with ExitStack() as _st:
        als = _st.enter_context(nc.sbuf_tensor("als", [128, 256], F32))
        wds = _st.enter_context(nc.sbuf_tensor("wds", [128, 16], F32))
        ones = _st.enter_context(nc.sbuf_tensor("ones", [1, 128], F32))
        brow = _st.enter_context(nc.sbuf_tensor("brow", [1, 4096], F32))
        crow = _st.enter_context(nc.sbuf_tensor("crow", [1, 4096], F32))
        dls = _st.enter_context(nc.sbuf_tensor("dls", [128, LQ], F32))
        us = _st.enter_context(nc.sbuf_tensor("us", [128, LQ], BF16))
        du = _st.enter_context(nc.sbuf_tensor("du", [128, LQ], F32))
        gg = _st.enter_context(nc.sbuf_tensor("gg", [128, LQ], F32))
        zz = _st.enter_context(nc.sbuf_tensor("zz", [128, LQ], F32))
        hh = _st.enter_context(nc.sbuf_tensor("hh", [128, LQ], F32))
        ya = _st.enter_context(nc.sbuf_tensor("ya", [128, LQ], F32))
        brep = _st.enter_context(nc.sbuf_tensor("brep", [128, LQ], F32))
        crep = _st.enter_context(nc.sbuf_tensor("crep", [128, LQ], F32))
        sks = _st.enter_context(nc.sbuf_tensor("sks", [128, LQ], BF16))
        ob = _st.enter_context(nc.sbuf_tensor("ob", [128, LQ], BF16))
        hmid = _st.enter_context(nc.sbuf_tensor("hmid", [128, 16], F32))
        pr = _st.enter_context(nc.psum_tensor("pr", [128, LQ], F32))
        S = _st.enter_context(nc.semaphore("S"))
        block = _st.enter_context(nc.Block())
        p = Prog(nc)
        p.S = S
        p.dma("sync", als[:], al[:, :])
        p.dma("sync", wds[:], wd[:, :])
        p.emit("vector", lambda e: e.memset(ones[:], 1.0))
        p.emit("scalar", lambda e: e.activation(
            out=als[:], in_=als[:], func=AF.Exp))
        p.emit("vector", lambda e: e.tensor_scalar(
            out=als[:], in0=als[:], scalar1=-1.0, scalar2=None, op0=ALU.mult))
        for et in range(16):
            for h in range(2):
                sl = slice(LQ * h, LQ * (h + 1))
                p.dma("sync", dls[:], dlt[et][:, sl])
                p.dma("sync", us[:], ut[et][:, sl])
                p.dma("sync", sks[:], skt[et][:, sl])
                p.emit("vector", lambda e: e.tensor_tensor(
                    out=du[:], in0=dls[:], in1=us[:], op=ALU.mult))
                p.emit("vector", lambda e: e.memset(ya[:], 0.0))
                for n in range(16):
                    idx = 16 * et + n
                    p.dma("sync", brow[0:1, 0:LQ], bt[n:n + 1, sl])
                    p.dma("sync", crow[0:1, 0:LQ], ct[n:n + 1, sl])
                    for seg in range(8):
                        p.emit("tensor", (lambda seg=seg: (lambda e: e.matmul(
                            out=pr[:, 512 * seg:512 * (seg + 1)],
                            lhsT=ones[:, :],
                            rhs=brow[0:1, 512 * seg:512 * (seg + 1)],
                            start=True, stop=True)))(),
                            inc=(1 if seg == 7 else 0), wait=(seg == 0))
                    p.emit("scalar", lambda e: e.copy(out=brep[:], in_=pr[:, 0:LQ]))
                    for seg in range(8):
                        p.emit("tensor", (lambda seg=seg: (lambda e: e.matmul(
                            out=pr[:, 512 * seg:512 * (seg + 1)],
                            lhsT=ones[:, :],
                            rhs=crow[0:1, 512 * seg:512 * (seg + 1)],
                            start=True, stop=True)))(),
                            inc=(1 if seg == 7 else 0), wait=(seg == 0))
                    p.emit("scalar", lambda e: e.copy(out=crep[:], in_=pr[:, 0:LQ]))
                    p.emit("scalar", (lambda idx=idx: (lambda e: e.activation(
                        out=gg[:], in_=dls[:], func=AF.Exp,
                        scale=als[:, idx:idx + 1])))())
                    p.emit("vector", lambda e: e.tensor_tensor(
                        out=zz[:], in0=du[:], in1=brep[:], op=ALU.mult))
                    if h == 0:
                        p.emit("vector", lambda e: e.tensor_tensor_scan(
                            out=hh[:], data0=gg[:], data1=zz[:], initial=0.0,
                            op0=ALU.mult, op1=ALU.add))
                    else:
                        p.emit("vector", (lambda n=n: (lambda e: e.tensor_tensor_scan(
                            out=hh[:], data0=gg[:], data1=zz[:],
                            initial=hmid[:, n:n + 1],
                            op0=ALU.mult, op1=ALU.add)))())
                    if h == 0:
                        p.emit("vector", (lambda n=n: (lambda e: e.tensor_copy(
                            out=hmid[:, n:n + 1], in_=hh[:, LQ - 1:LQ])))())
                    p.emit("gpsimd", lambda e: e.tensor_tensor(
                        out=gg[:], in0=hh[:], in1=crep[:], op=ALU.mult))
                    p.emit("gpsimd", lambda e: e.tensor_add(
                        out=ya[:], in0=ya[:], in1=gg[:]))
                p.emit("vector", (lambda et=et: (lambda e: e.scalar_tensor_tensor(
                    out=ya[:], in0=us[:], scalar=wds[:, et:et + 1], in1=ya[:],
                    op0=ALU.mult, op1=ALU.add)))())
                p.emit("vector", lambda e: e.tensor_tensor(
                    out=ob[:], in0=ya[:], in1=sks[:], op=ALU.mult))
                p.dma("gpsimd", ot[et][:, sl], ob[:])
        p.finalize(block)


def build_gateup(nc):
    xT = nc.dram_tensor("xT", [2048, 4096], BF16, kind="ExternalInput").ap()
    wg = nc.dram_tensor("wg", [2048, 4096], BF16, kind="ExternalInput").ap()
    wu = nc.dram_tensor("wu", [2048, 4096], BF16, kind="ExternalInput").ap()
    o = nc.dram_tensor("o", [4096, 4096], BF16, kind="ExternalOutput").ap()
    xt = xT.rearrange("(t p) c -> t p c", p=128)
    wgt = wg.rearrange("(t p) c -> t p c", p=128)
    wut = wu.rearrange("(t p) c -> t p c", p=128)
    ot = o.rearrange("(t p) c -> t p c", p=128)

    with ExitStack() as _st:
        xs = _st.enter_context(nc.sbuf_tensor("xs", [128, 16, 4096], BF16))
        wgs = _st.enter_context(nc.sbuf_tensor("wgs", [128, 16, 128], BF16))
        wus = _st.enter_context(nc.sbuf_tensor("wus", [128, 16, 128], BF16))
        gsb = _st.enter_context(nc.sbuf_tensor("gs", [128, 512], F32))
        usb = _st.enter_context(nc.sbuf_tensor("usb", [128, 512], F32))
        tb = _st.enter_context(nc.sbuf_tensor("t", [128, 512], F32))
        s2b = _st.enter_context(nc.sbuf_tensor("s2", [128, 512], F32))
        pbb = _st.enter_context(nc.sbuf_tensor("pb", [128, 512], F32))
        ib = _st.enter_context(nc.sbuf_tensor("ib", [128, 512], BF16))
        pg = _st.enter_context(nc.psum_tensor("pg", [128, 512], F32))
        pu = _st.enter_context(nc.psum_tensor("pu", [128, 512], F32))
        S = _st.enter_context(nc.semaphore("S"))
        block = _st.enter_context(nc.Block())
        p = Prog(nc)
        p.S = S
        _consts(nc, p, _st, [0.7978845608028654])
        for k in range(16):
            p.dma("sync", xs[:, k, :], xt[k])
        for m in range(32):
            for k in range(16):
                p.dma("sync", wgs[:, k, :], wgt[k][:, 128 * m:128 * (m + 1)])
                p.dma("sync", wus[:, k, :], wut[k][:, 128 * m:128 * (m + 1)])
            for c in range(8):
                _mm_chain(p, pg[:, :], [wgs[:, k, :] for k in range(16)],
                          [xs[:, k, 512 * c:512 * (c + 1)] for k in range(16)])
                _mm_chain(p, pu[:, :], [wus[:, k, :] for k in range(16)],
                          [xs[:, k, 512 * c:512 * (c + 1)] for k in range(16)])
                p.emit("scalar", lambda e: e.copy(out=gsb[:], in_=pg[:, :]))
                p.emit("scalar", lambda e: e.copy(out=usb[:], in_=pu[:, :]))
                p.emit("scalar", lambda e: e.activation(
                    out=tb[:], in_=gsb[:], func=AF.Square))
                p.emit("vector", lambda e: e.tensor_scalar(
                    out=s2b[:], in0=tb[:], scalar1=0.044715, scalar2=1.0,
                    op0=ALU.mult, op1=ALU.add))
                p.emit("vector", lambda e: e.tensor_mul(
                    out=s2b[:], in0=s2b[:], in1=gsb[:]))
                p.emit("scalar", lambda e: e.activation(
                    out=tb[:], in_=s2b[:], func=AF.Tanh,
                    scale=0.7978845608028654))
                p.emit("vector", lambda e: e.scalar_tensor_tensor(
                    out=pbb[:], in0=gsb[:], scalar=0.5, in1=usb[:],
                    op0=ALU.mult, op1=ALU.mult))
                p.emit("vector", lambda e: e.tensor_scalar(
                    out=tb[:], in0=tb[:], scalar1=1.0, scalar2=None,
                    op0=ALU.add))
                p.emit("vector", lambda e: e.tensor_tensor(
                    out=ib[:], in0=tb[:], in1=pbb[:], op=ALU.mult))
                p.dma("gpsimd", ot[m][:, 512 * c:512 * (c + 1)], ib[:])
        p.finalize(block)


def build_down(nc):
    it_ = nc.dram_tensor("it", [16384, 512], BF16, kind="ExternalInput").ap()
    wd = nc.dram_tensor("wd", [16384, 2048], BF16, kind="ExternalInput").ap()
    o = nc.dram_tensor("o", [2048, 512], BF16, kind="ExternalOutput").ap()
    itt = it_.rearrange("(t p) c -> t p c", p=128)
    wdt = wd.rearrange("(t p) c -> t p c", p=128)
    ot = o.rearrange("(t p) c -> t p c", p=128)

    with ExitStack() as _st:
        isb = _st.enter_context(nc.sbuf_tensor("isb", [128, 512], BF16))
        wsb = _st.enter_context(nc.sbuf_tensor("wsb", [128, 1024], BF16))
        ev = _st.enter_context(nc.sbuf_tensor("ev", [128, 512], BF16))
        ps = _st.enter_context(nc.psum_tensor("ps", [128, 4096], F32))
        S = _st.enter_context(nc.semaphore("S"))
        block = _st.enter_context(nc.Block())
        p = Prog(nc)
        p.S = S
        for dh in range(2):
            for k in range(128):
                p.dma("sync", isb[:], itt[k])
                p.dma("sync", wsb[:], wdt[k][:, 1024 * dh:1024 * (dh + 1)])
                for d in range(8):
                    p.emit("tensor", (lambda d=d, k=k: (lambda e: e.matmul(
                        out=ps[:, 512 * d:512 * (d + 1)],
                        lhsT=wsb[:, 128 * d:128 * (d + 1)],
                        rhs=isb[:, :], start=(k == 0), stop=(k == 127))))(),
                        inc=(1 if d == 7 else 0), wait=(d == 0))
            for d in range(8):
                p.emit("scalar", (lambda d=d: (lambda e: e.activation(
                    out=ev[:], in_=ps[:, 512 * d:512 * (d + 1)],
                    func=AF.Silu)))())
                p.dma("gpsimd", ot[8 * dh + d], ev[:])
        p.finalize(block)


def build_out(nc):
    y3 = nc.dram_tensor("y3", [4096, 4096], BF16, kind="ExternalInput").ap()
    wo = nc.dram_tensor("wo", [4096, 2048], F32, kind="ExternalInput").ap()
    g = nc.dram_tensor("g", [4096, 2048], BF16, kind="ExternalInput").ap()
    x = nc.dram_tensor("x", [4096, 2048], F32, kind="ExternalInput").ap()
    o = nc.dram_tensor("o", [4096, 2048], F32, kind="ExternalOutput").ap()
    y3t = y3.rearrange("(t p) c -> t p c", p=128)
    wot = wo.rearrange("(t p) c -> t p c", p=128)
    gt = g.rearrange("(t p) c -> t p c", p=128)
    xt = x.rearrange("(t p) c -> t p c", p=128)
    ot = o.rearrange("(t p) c -> t p c", p=128)

    with ExitStack() as _st:
        wf = _st.enter_context(nc.sbuf_tensor("wf", [128, 2048], F32))
        wb = _st.enter_context(nc.sbuf_tensor("wb", [128, 32, 2048], BF16))
        ys = _st.enter_context(nc.sbuf_tensor("ys", [128, 32, 128], BF16))
        gsb = _st.enter_context(nc.sbuf_tensor("gsb", [128, 2048], BF16))
        xsb = _st.enter_context(nc.sbuf_tensor("xsb", [128, 2048], F32))
        sv = _st.enter_context(nc.sbuf_tensor("sv", [128, 512], F32))
        ob = _st.enter_context(nc.sbuf_tensor("ob", [128, 2048], F32))
        ps = _st.enter_context(nc.psum_tensor("ps", [128, 512], F32))
        S = _st.enter_context(nc.semaphore("S"))
        block = _st.enter_context(nc.Block())
        p = Prog(nc)
        p.S = S
        for k in range(32):
            p.dma("sync", wf[:], wot[k])
            p.emit("vector", (lambda k=k: (lambda e: e.tensor_copy(
                out=wb[:, k, :], in_=wf[:])))())
        for lt in range(32):
            for k in range(32):
                p.dma("sync", ys[:, k, :], y3t[k][:, 128 * lt:128 * (lt + 1)])
            p.dma("sync", gsb[:], gt[lt])
            p.dma("sync", xsb[:], xt[lt])
            for c in range(4):
                _mm_chain(p, ps[:, :], [ys[:, k, :] for k in range(32)],
                          [wb[:, k, 512 * c:512 * (c + 1)] for k in range(32)])
                p.emit("scalar", lambda e: e.copy(out=sv[:], in_=ps[:, :]))
                p.emit("vector", (lambda c=c: (lambda e: e.tensor_tensor(
                    out=sv[:], in0=sv[:], in1=gsb[:, 512 * c:512 * (c + 1)],
                    op=ALU.mult)))())
                p.emit("vector", (lambda c=c: (lambda e: e.tensor_add(
                    out=ob[:, 512 * c:512 * (c + 1)], in0=sv[:],
                    in1=xsb[:, 512 * c:512 * (c + 1)])))())
            p.dma("gpsimd", ot[lt], ob[:])
        p.finalize(block)


def kernel(**inp):
    C = np.ascontiguousarray

    def f32(a):
        return C(np.asarray(a, dtype=np.float32))

    x = f32(inp["L0_attn_x"])
    norm_w = f32(inp["L0_attn_norm_w"])
    W_skip = f32(inp["L0_attn_W_skip"])
    W_in = f32(inp["L0_attn_W_in"])
    conv_w = f32(inp["L0_attn_conv_w"])
    conv_b = f32(inp["L0_attn_conv_b"])
    Wd1 = f32(inp["L0_attn_Wd1"])
    Wd2 = f32(inp["L0_attn_Wd2"])
    bd2 = f32(inp["L0_attn_bd2"])
    WB = f32(inp["L0_attn_WB"])
    WC = f32(inp["L0_attn_WC"])
    A_log = f32(inp["L0_attn_A_log"])
    W_D = f32(inp["L0_attn_W_D"])
    W_out = f32(inp["L0_attn_W_out"])
    gWp = np.asarray(inp["L0_ffn_gate_W_packed"], dtype=np.int32)
    gSc = f32(inp["L0_ffn_gate_scales"])
    uWp = np.asarray(inp["L0_ffn_up_W_packed"], dtype=np.int32)
    uSc = f32(inp["L0_ffn_up_scales"])
    dWp = np.asarray(inp["L0_ffn_down_W_packed"], dtype=np.int32)
    dSc = f32(inp["L0_ffn_down_scales"])

    maps = []
    for c in range(8):
        maps.append({
            "gp": C(gWp[2048 * c:2048 * (c + 1)]),
            "up": C(uWp[2048 * c:2048 * (c + 1)]),
            "dp": C(dWp[256 * c:256 * (c + 1)]),
            "gs": C(np.repeat(gSc[2048 * c:2048 * (c + 1)], 32, axis=1)),
            "us": C(np.repeat(uSc[2048 * c:2048 * (c + 1)], 32, axis=1)),
            "ds": C(np.repeat(dSc[256 * c:256 * (c + 1)], 32, axis=1)),
        })
    r = _run("dq", build_dequant, maps)
    Wg = np.concatenate([r[c]["og"] for c in range(8)], axis=0)
    Wu = np.concatenate([r[c]["ou"] for c in range(8)], axis=0)
    Wd = np.concatenate([r[c]["od"] for c in range(8)], axis=0)
    WgT = C(Wg.T)
    WuT = C(Wu.T)
    WdT = C(Wd.T)

    xf = x.reshape(32768, 2048)
    nwrep = C(np.repeat(norm_w[None, :], 128, axis=0))
    maps = [{"x": C(xf[4096 * c:4096 * (c + 1)]), "nw": nwrep} for c in range(8)]
    r = _run("ln", build_ln, maps)
    xn1 = np.concatenate([r[c]["o1"] for c in range(8)], axis=0)
    xn2 = np.concatenate([r[c]["o2"] for c in range(8)], axis=0)
    xn2T = [C(xn2.reshape(4, 8192, 2048)[b].T) for b in range(4)]
    xn1T = [[C(xn1.reshape(4, 2, 4096, 2048)[b, h].T) for h in range(2)]
            for b in range(4)]

    maps = []
    for c in range(8):
        b, pp = c // 2, c % 2
        maps.append({"wT": C(W_in[2048 * pp:2048 * (pp + 1)].T),
                     "xT": xn2T[b]})
    r = _run("pu", lambda nc: build_proj(nc, silu=False), maps)
    u_pre = [r[c]["o"] for c in range(8)]

    maps = []
    for c in range(8):
        b, pp = c // 2, c % 2
        maps.append({"wT": C(W_skip[2048 * pp:2048 * (pp + 1)].T),
                     "xT": xn2T[b]})
    r = _run("ps", lambda nc: build_proj(nc, silu=True), maps)
    sskip = [r[c]["o"] for c in range(8)]

    maps = []
    for c in range(8):
        b, pp = c // 2, c % 2
        cw = conv_w[2048 * pp:2048 * (pp + 1), 0, :]
        cw = C(cw.reshape(16, 128, 4).transpose(1, 0, 2).reshape(128, 64))
        cb = C(conv_b[2048 * pp:2048 * (pp + 1)].reshape(16, 128).T)
        maps.append({"u": C(u_pre[c]), "cw": cw, "cb": cb})
    r = _run("cv", build_conv, maps)
    u_bf = [r[c]["o"] for c in range(8)]

    maps = []
    for c in range(8):
        b, h = c // 2, c % 2
        uf = np.concatenate([u_bf[2 * b], u_bf[2 * b + 1]], axis=0)
        maps.append({"u": C(uf[:, 4096 * h:4096 * (h + 1)]),
                     "w1": C(Wd1.T), "wb": C(WB.T), "wc": C(WC.T)})
    r = _run("tbc", build_tbc, maps)
    t1T = [np.concatenate([r[2 * b]["t1"], r[2 * b + 1]["t1"]], axis=1)
           for b in range(4)]
    BT = [np.concatenate([r[2 * b]["bo"], r[2 * b + 1]["bo"]], axis=1)
          for b in range(4)]
    CT = [np.concatenate([r[2 * b]["co"], r[2 * b + 1]["co"]], axis=1)
          for b in range(4)]

    maps = []
    for c in range(8):
        b, pp = c // 2, c % 2
        bdc = C(bd2[2048 * pp:2048 * (pp + 1)].reshape(16, 128).T)
        maps.append({"t1": C(t1T[b]),
                     "w2": C(Wd2[2048 * pp:2048 * (pp + 1)].T),
                     "bd": bdc})
    r = _run("dl", build_delta, maps)
    delta = [r[c]["o"] for c in range(8)]

    maps = []
    for c in range(8):
        b, pp = c // 2, c % 2
        alc = A_log[2048 * pp:2048 * (pp + 1)]
        alc = C(alc.reshape(16, 128, 16).transpose(1, 0, 2).reshape(128, 256))
        wdc = C(W_D[2048 * pp:2048 * (pp + 1)].reshape(16, 128).T)
        maps.append({"dl": C(delta[c]), "u": C(u_bf[c]), "bt": C(BT[b]),
                     "ct": C(CT[b]), "al": alc, "wd": wdc,
                     "sk": C(sskip[c])})
    r = _run("sc", build_scan, maps)
    y3 = [r[c]["o"] for c in range(8)]

    inter = [[None] * 4 for _ in range(8)]
    for q in range(4):
        maps = []
        for c in range(8):
            b, h = c // 2, c % 2
            maps.append({"xT": xn1T[b][h],
                         "wg": C(WgT[:, 4096 * q:4096 * (q + 1)]),
                         "wu": C(WuT[:, 4096 * q:4096 * (q + 1)])})
        r = _run("gu", build_gateup, maps)
        for c in range(8):
            inter[c][q] = r[c]["o"]
    interf = [np.concatenate(inter[c], axis=0) for c in range(8)]

    g = [[None] * 8 for _ in range(8)]
    for j in range(8):
        maps = []
        for c in range(8):
            maps.append({"it": C(interf[c][:, 512 * j:512 * (j + 1)]),
                         "wd": WdT})
        r = _run("dn", build_down, maps)
        for c in range(8):
            g[c][j] = r[c]["o"]
    gful = [C(np.concatenate(g[c], axis=1).T) for c in range(8)]

    WoT = C(W_out.T)
    maps = []
    for c in range(8):
        b, h = c // 2, c % 2
        yf = np.concatenate([y3[2 * b], y3[2 * b + 1]], axis=0)
        maps.append({"y3": C(yf[:, 4096 * h:4096 * (h + 1)]),
                     "wo": WoT, "g": gful[c],
                     "x": C(x.reshape(4, 2, 4096, 2048)[b, h])})
    r = _run("out", build_out, maps)
    out = np.stack([np.concatenate([r[2 * b]["o"], r[2 * b + 1]["o"]], axis=0)
                    for b in range(4)], axis=0)
    return np.ascontiguousarray(out.astype(np.float32))
